# version 03 (final)
# baseline (speedup 1.0000x reference)
import os
import numpy as np
import concourse.bass as bass
import concourse.mybir as mybir

F32 = mybir.dt.float32
BF16 = mybir.dt.bfloat16
AF = mybir.ActivationFunctionType
OP = mybir.AluOpType
AX = mybir.AxisListType.X

NEG = float(np.finfo(np.float32).min)
N = 2048
NC = 8
NT5 = 4
NT1 = 16
HEADS = 16
HD = 64
ENGS = ("sync", "gpsimd", "tensor", "vector", "scalar")
SLOT = 512


class _Op:
    __slots__ = ("i", "eng", "fn", "dma", "tag", "deps", "needs_inc", "sem", "val")

    def __init__(self, i, eng, fn, dma, tag):
        self.i, self.eng, self.fn, self.dma, self.tag = i, eng, fn, dma, tag
        self.deps = set()
        self.needs_inc = False
        self.sem = None
        self.val = 0


def _intervals(ap):
    t = ap.tensor
    name = t.name
    shape = list(t.shape)
    try:
        space = ap.space
    except Exception:
        space = None
    if str(space) in ("MemorySpace.SBUF", "MemorySpace.PSUM"):
        rowlen = 1
        for s in shape[1:]:
            rowlen *= s
        off = ap.offset
        lo = off % rowlen
        span = 0
        for st, cnt in ap.ap:
            if st != rowlen:
                span += (cnt - 1) * st
        return (name, lo, lo + span + 1)
    return (name, 0, 1 << 60)


class KB:

    def __init__(self, nc):
        self.nc = nc
        self.ops = []
        self.tag = "t0"
        self.led = {}
        self.ro_names = set()

    def settag(self, tag):
        self.tag = tag

    def _keys(self, ap):
        name, lo, hi = _intervals(ap)
        if hi - lo > (1 << 40):
            return [(name, -1)]
        return [(name, s) for s in range(lo // SLOT, (hi - 1) // SLOT + 1)]

    def op(self, eng, fn, reads=(), writes=(), dma=False):
        o = _Op(len(self.ops), eng, fn, dma, self.tag)
        for ap in reads:
            if ap is None or ap.tensor.name in self.ro_names:
                continue
            for k in self._keys(ap):
                ent = self.led.get(k)
                if ent is not None and ent[0] is not None:
                    o.deps.add(ent[0])
        for ap in writes:
            for k in self._keys(ap):
                ent = self.led.get(k)
                if ent is not None:
                    if ent[0] is not None:
                        o.deps.add(ent[0])
                    o.deps.update(ent[1])
        self.ops.append(o)
        for ap in reads:
            if ap is None:
                continue
            if getattr(ap.tensor, "kind", None) == "ExternalInput":
                continue
            for k in self._keys(ap):
                self.led.setdefault(k, [None, []])[1].append(o.i)
        for ap in writes:
            for k in self._keys(ap):
                self.led[k] = [o.i, []]
        return o.i

    def barrier_all(self, eng="sync"):
        last = {}
        for o in self.ops:
            last[o.eng] = o.i
        o = _Op(len(self.ops), eng, lambda e: None, False, self.tag)
        o.deps = set(last.values())
        self.ops.append(o)
        self.led = {}
        return o.i

    def finalize(self):
        ops = self.ops
        for o in ops:
            if o.eng == "tensor":
                o.deps = {d for d in o.deps if ops[d].eng != "tensor"}
        for o in ops:
            if o.dma:
                o.needs_inc = True
        for o in ops:
            per = {}
            for d in o.deps:
                e = ops[d].eng
                if per.get(e, -1) < d:
                    per[e] = d
            o.deps = set(per.values())
            for d in o.deps:
                ops[d].needs_inc = True
        semv = {}
        for o in ops:
            if o.needs_inc:
                key = (o.eng, o.tag)
                semv.setdefault(key, 0)
                semv[key] += 16 if o.dma else 1
                o.sem = "S_%s_%s" % key
                o.val = semv[key]
        sems = {}
        for key in semv:
            nm = "S_%s_%s" % key
            sems[nm] = self.nc.alloc_semaphore(nm)
        streams = {e: [] for e in ENGS}
        for o in ops:
            streams[o.eng].append(o)
        with self.nc.allow_low_precision(reason="bf16 transpose dst / bf16 psum views"), \
             self.nc.Block() as block:
            for e in ENGS:
                def runner(engine, e=e):
                    water = {}
                    for o in streams[e]:
                        for d in sorted(o.deps):
                            do = ops[d]
                            if do.sem is None:
                                continue
                            if water.get(do.sem, 0) >= do.val:
                                continue
                            engine.wait_ge(sems[do.sem], do.val)
                            water[do.sem] = do.val
                        inst = o.fn(engine)
                        if o.needs_inc:
                            if inst is None:
                                engine.sem_inc(sems[o.sem], 16 if o.dma else 1)
                            else:
                                inst.then_inc(sems[o.sem], 16 if o.dma else 1)
                getattr(block, e)(runner)
        return len(ops), sum(1 for o in ops if o.needs_inc)



def stage_host(inputs):
    g = inputs
    st = {}
    f32c = lambda a: np.ascontiguousarray(np.asarray(a), dtype=np.float32)
    st["wqkvT0"] = f32c(np.asarray(g["L0_attn_w_qkv"]).T)
    st["woutT0"] = f32c(np.asarray(g["L0_attn_w_out"]).T)
    st["bout0"] = f32c(np.asarray(g["L0_attn_b_out"]).reshape(NC, 128).T)
    rfT = np.asarray(g["L0_attn_random_features"]).T
    bd = np.zeros((128, 128), np.float32)
    bd[0:64, 0:64] = rfT
    bd[64:128, 64:128] = rfT
    st["bdrf"] = bd
    st["rfT2"] = f32c(np.vstack([rfT, rfT]))
    st["wr0"] = f32c(g["L0_ffn_Wr"])
    st["w1_0"] = f32c(g["L0_ffn_W1"])
    st["w2_0"] = f32c(g["L0_ffn_W2"])
    st["w3_0"] = f32c(g["L0_ffn_W3"])
    st["cattnw"] = f32c(g["L1_attn_c_attn_w"])
    st["cattnb"] = f32c(np.asarray(g["L1_attn_c_attn_b"]).reshape(24, 128).T)
    st["aprojw"] = f32c(g["L1_attn_attn_proj_w"])
    st["aprojb"] = f32c(np.asarray(g["L1_attn_attn_proj_b"]).reshape(NC, 128).T)
    st["qattnw"] = f32c(g["L1_attn_q_attn_w"])
    st["qattnb"] = f32c(np.asarray(g["L1_attn_q_attn_b"]).reshape(NC, 128).T)
    st["xkvw"] = f32c(g["L1_attn_x_kv_w"])
    st["xkvb"] = f32c(np.asarray(g["L1_attn_x_kv_b"]).reshape(16, 128).T)
    st["xprojw"] = f32c(g["L1_attn_x_proj_w"])
    st["xprojb"] = f32c(np.asarray(g["L1_attn_x_proj_b"]).reshape(NC, 128).T)
    st["fcw"] = f32c(g["L1_attn_fc_w"])
    st["fcb"] = f32c(np.asarray(g["L1_attn_fc_b"]).reshape(32, 128).T)
    st["mprojw"] = f32c(g["L1_attn_mlp_proj_w"])
    st["mprojb"] = f32c(np.asarray(g["L1_attn_mlp_proj_b"]).reshape(NC, 128).T)
    for nm, key in (("ln1g", "L1_attn_ln1_g"), ("ln1b", "L1_attn_ln1_b"),
                    ("lnxg", "L1_attn_lnx_g"), ("lnxb", "L1_attn_lnx_b"),
                    ("ln2g", "L1_attn_ln2_g"), ("ln2b", "L1_attn_ln2_b")):
        st[nm] = f32c(np.asarray(g[key]).reshape(NC, 128).T)
    st["encT"] = f32c(np.asarray(g["L1_attn_encoder_hidden_states"])[0].T)
    st["scorer"] = f32c(np.asarray(g["L1_attn_scorer_logits"])[0].reshape(16, 128).T)
    st["gate1"] = f32c(g["L1_ffn_gate"])
    st["cattnbV"] = f32c(np.asarray(g["L1_attn_c_attn_b"])[None, 2048:3072])
    st["xkvbV"] = f32c(np.asarray(g["L1_attn_x_kv_b"])[None, 1024:2048])
    st["ew1"] = f32c(g["L1_ffn_experts_w1"])
    st["ew2"] = f32c(g["L1_ffn_experts_w2"])
    st["ew3"] = f32c(g["L1_ffn_experts_w3"])
    st["ident"] = np.eye(128, dtype=np.float32)
    r = np.arange(128)
    m = np.zeros((128, 128), np.float32)
    m[r[:, None] < r[None, :]] = NEG
    st["mask_self"] = m
    m2 = np.zeros((128, 128), np.float32)
    m2[r[:, None] + 2 < r[None, :]] = NEG
    st["mask_c2"] = m2
    msp = np.full((128, 128), NEG, np.float32)
    msp[126, 0] = 0.0
    msp[127, 0] = 0.0
    msp[127, 1] = 0.0
    st["mask_sp"] = msp
    Ec = np.zeros((16, NC * 128), np.float32)
    for c in range(NC):
        Ec[2 * c, c * 128:c * 128 + 64] = 1.0
        Ec[2 * c + 1, c * 128 + 64:c * 128 + 128] = 1.0
    st["Ec"] = Ec
    E8 = np.zeros((8, 8 * 128), np.float32)
    for e in range(8):
        E8[e, e * 128:(e + 1) * 128] = 1.0
    st["E8"] = E8
    E2 = np.zeros((128, NC * 16), np.float32)
    for c in range(NC):
        E2[0:64, c * 16 + 2 * c] = 1.0
        E2[64:128, c * 16 + 2 * c + 1] = 1.0
    st["E2seg"] = E2
    xT = np.ascontiguousarray(np.asarray(g["L0_attn_x"]).transpose(0, 2, 1))
    return st, xT


SHARED_SHAPES = [
    ("wqkvT0", [1024, 3072]), ("woutT0", [1024, 1024]), ("bout0", [128, NC]),
    ("bdrf", [128, 128]), ("rfT2", [128, 64]), ("wr0", [1024, 8]),
    ("w1_0", [8, 1024, 1536]), ("w2_0", [8, 1536, 1024]), ("w3_0", [8, 1024, 1536]),
    ("cattnw", [1024, 3072]), ("cattnb", [128, 24]),
    ("aprojw", [1024, 1024]), ("aprojb", [128, NC]),
    ("qattnw", [1024, 1024]), ("qattnb", [128, NC]),
    ("xkvw", [1024, 2048]), ("xkvb", [128, 16]),
    ("xprojw", [1024, 1024]), ("xprojb", [128, NC]),
    ("fcw", [1024, 4096]), ("fcb", [128, 32]),
    ("mprojw", [4096, 1024]), ("mprojb", [128, NC]),
    ("ln1g", [128, NC]), ("ln1b", [128, NC]), ("lnxg", [128, NC]), ("lnxb", [128, NC]),
    ("ln2g", [128, NC]), ("ln2b", [128, NC]),
    ("encT", [1024, N]), ("scorer", [128, 16]),
    ("gate1", [1024, 8]), ("cattnbV", [1, 1024]), ("xkvbV", [1, 1024]),
    ("ew1", [8, 1024, 4096]), ("ew2", [8, 4096, 1024]), ("ew3", [8, 1024, 4096]),
    ("E8", [8, 8 * 128]), ("ident", [128, 128]), ("mask_self", [128, 128]), ("mask_c2", [128, 128]),
    ("mask_sp", [128, 128]), ("Ec", [16, NC * 128]), ("E2seg", [128, NC * 16]),
]


class Bufs:
    _SHAPES = dict(SHARED_SHAPES)

    def __getattr__(self, nm):
        if nm in Bufs._SHAPES:
            ap = self._nc.dram_tensor(nm, Bufs._SHAPES[nm], F32, kind="ExternalInput").ap()
            object.__setattr__(self, nm, ap)
            return ap
        raise AttributeError(nm)


def declare(nc, dump=False):
    B = Bufs()
    object.__setattr__(B, "_nc", nc)
    d = lambda name, shape: nc.dram_tensor(name, shape, F32, kind="ExternalInput").ap()
    B.xT = d("xT", [2, 1024, N])
    B.outT = nc.dram_tensor("outT", [2, 1024, N], F32, kind="ExternalOutput").ap()
    B.dumpT = nc.dram_tensor("dumpT", [128, NC * N], F32, kind="ExternalOutput").ap() if dump else None
    B.spill = nc.dram_tensor("spill", [1024, N], F32, kind="Internal").ap()

    s = nc.alloc_sbuf_tensor
    B.X = s("X", [128, NC * N], F32)
    B.XN = s("XN", [128, NC * N], BF16)
    B.TA = s("TA", [128, NC * N], BF16)
    B.TC = s("TC", [128, NC * N // 2], BF16)
    B.WT = s("WT", [128, 2 * 8192], BF16)
    B.SC = s("SC", [128, 1024], F32)
    B.CMB = s("CMB", [8, 2048], BF16)
    B.E8S = s("E8S", [8, 1024], BF16)
    B.VBR = s("VBR", [1, 1024], BF16)
    B.SSCb = s("SSCb", [128, 16], BF16)
    B.NM = s("NM", [128, 16], F32)
    B.ZR = s("ZR", [1, 128], F32)
    B.idnb = s("idnb", [128, 128], BF16)
    B.z128 = s("z128", [1, 128], BF16)
    B.NK = s("NK", [128, 256], F32)
    B.NQ = s("NQ", [16, 512], BF16)
    B.RN = s("RN", [16, 512], BF16)
    B.KV = s("KV", [128, 1024], BF16)
    B.KSA = s("KSA", [128, 128], BF16)
    B.SR = s("SR", [128, 1024], F32)
    B.SB = s("SB", [128, 1024], BF16)
    B.idn = s("idn", [128, 128], F32)
    B.msk = s("msk", [128, 3 * 128], F32)
    B.EcS = s("EcS", [16, NC * 128], BF16)
    B.E2S = s("E2S", [128, NC * 16], BF16)
    B.bdrfS = s("bdrfS", [128, 128], BF16)
    B.rfT2S = s("rfT2S", [128, 64], BF16)
    B.ones1f = s("ones1f", [1, 128], F32)
    B.onesc = s("onesc", [128, 1], BF16)
    B.eps = s("eps", [1, 1], F32)
    B.BIAS = s("BIAS", [128, 120], F32)
    B.LNP = s("LNP", [128, 6 * NC], F32)
    B.scorerS = s("scorerS", [128, 16], F32)
    B.psS = nc.alloc_psum_tensor("psS", [128, 2048], F32)
    B.psA = nc.alloc_psum_tensor("psA", [128, 512], F32)
    B.psB = nc.alloc_psum_tensor("psB", [128, 512], F32)
    B.psStat = nc.alloc_psum_tensor("psStat", [128, 1024], F32)
    return B


BIAS_SLOTS = {
    "bout0": (0, NC), "cattnb": (8, 24), "aprojb": (32, NC), "qattnb": (40, NC),
    "xkvb": (48, 16), "xprojb": (64, NC), "fcb": (72, 32), "mprojb": (104, NC),
}
LN_ORDER = ("ln1g", "ln1b", "lnxg", "lnxb", "ln2g", "ln2b")


def fm(buf, c, j, w=512, n=N):
    return buf[:, c * n + j * w:c * n + (j + 1) * w]


def ph_const_load(kb, B):
    kb.settag("cst")
    def dma(dst, src):
        kb.op("gpsimd", lambda g, d=dst, s=src: g.dma_start(out=d, in_=s),
              reads=[src], writes=[dst], dma=True)
    dma(B.idn[:, :], B.ident[:, :])
    dma(B.msk[:, 0:128], B.mask_self[:, :])
    dma(B.msk[:, 128:256], B.mask_c2[:, :])
    dma(B.msk[:, 256:384], B.mask_sp[:, :])
    dma(B.EcS[:, :], B.Ec[:, :])
    dma(B.E2S[:, :], B.E2seg[:, :])
    dma(B.E8S[:, :], B.E8[:, :])
    dma(B.idnb[:, :], B.ident[:, :])
    dma(B.bdrfS[:, :], B.bdrf[:, :])
    dma(B.rfT2S[:, :], B.rfT2[:, :])
    for nm, (o, w) in BIAS_SLOTS.items():
        dma(B.BIAS[:, o:o + w], getattr(B, nm)[:, :])
    for i, nm in enumerate(LN_ORDER):
        dma(B.LNP[:, i * NC:(i + 1) * NC], getattr(B, nm)[:, :])
    dma(B.scorerS[:, :], B.scorer[:, :])
    kb.op("scalar", lambda sc: sc.activation(B.scorerS[:, :], B.scorerS[:, :], AF.Exp),
          reads=[B.scorerS[:, :]], writes=[B.scorerS[:, :]])
    kb.op("vector", lambda v: v.tensor_copy(B.SSCb[:, :], B.scorerS[:, :]),
          reads=[B.scorerS[:, :]], writes=[B.SSCb[:, :]])
    kb.op("vector", lambda v: v.memset(B.ones1f[:, :], 1.0), writes=[B.ones1f[:, :]])
    kb.op("vector", lambda v: v.memset(B.onesc[:, :], 1.0), writes=[B.onesc[:, :]])
    kb.op("vector", lambda v: v.memset(B.z128[:, :], 0.0), writes=[B.z128[:, :]])
    kb.op("vector", lambda v: v.memset(B.eps[:, :], 1e-5), writes=[B.eps[:, :]])


def ph_load_x(kb, B, s):
    kb.op("sync", lambda sy: sy.dma_start(
        out=B.X[:, :].rearrange("p (c n) -> p c n", c=NC),
        in_=B.xT[s].rearrange("(c p) n -> p c n", p=128)),
        reads=[B.xT[s]], writes=[B.X[:, :]], dma=True)


def ph_layernorm(kb, B, tag, src, dst_bf=None, dst_f32=None, g_ap=None, b_ap=None):
    kb.settag(tag)
    for j in range(NT5):
        for c in range(NC):
            xb = B.SB[:, 0:512]
            xsq = B.SB[:, 512:1024]
            kb.op("vector", lambda v, c=c, j=j, xb=xb: v.tensor_copy(xb, src(c, j)),
                  reads=[src(c, j)], writes=[xb])
            kb.op("scalar", lambda sc, xb=xb, xsq=xsq: sc.activation(xsq, xb, AF.Square),
                  reads=[xb], writes=[xsq])
            kb.op("tensor", lambda te, xb=xb, c=c: te.matmul(
                B.psStat[0:1, 0:512], B.onesc[:, :], xb, start=(c == 0), stop=(c == NC - 1)),
                reads=[B.onesc[:, :], xb], writes=[B.psStat[0:1, 0:512]])
            kb.op("tensor", lambda te, xsq=xsq, c=c: te.matmul(
                B.psStat[0:1, 512:1024], B.onesc[:, :], xsq, start=(c == 0), stop=(c == NC - 1)),
                reads=[B.onesc[:, :], xsq], writes=[B.psStat[0:1, 512:1024]])
        r_mean = B.SR[0:1, 0:512]
        r_var = B.SR[0:1, 512:1024]
        msq = B.SC[0:1, 0:512]
        kb.op("vector", lambda v: v.tensor_scalar(r_mean, B.psStat[0:1, 0:512], 1.0 / 1024, None, OP.mult),
              reads=[B.psStat[0:1, 0:512]], writes=[r_mean])
        kb.op("vector", lambda v: v.tensor_scalar(r_var, B.psStat[0:1, 512:1024], 1.0 / 1024, None, OP.mult),
              reads=[B.psStat[0:1, 512:1024]], writes=[r_var])
        kb.op("scalar", lambda sc: sc.activation(msq, r_mean, AF.Square),
              reads=[r_mean], writes=[msq])
        kb.op("vector", lambda v: v.tensor_sub(r_var, r_var, msq),
              reads=[r_var, msq], writes=[r_var])
        kb.op("scalar", lambda sc: sc.activation(r_var, r_var, AF.Ln, bias=B.eps[0:1, 0:1]),
              reads=[r_var, B.eps[0:1, 0:1]], writes=[r_var])
        kb.op("scalar", lambda sc: sc.activation(r_var, r_var, AF.Exp, scale=-0.5),
              reads=[r_var], writes=[r_var])
        kb.op("vector", lambda v: v.tensor_mul(r_mean, r_mean, r_var),
              reads=[r_mean, r_var], writes=[r_mean])
        kb.op("tensor", lambda te: te.matmul(B.psStat[:, 0:512], B.ones1f[0:1, 0:128], r_var,
                                             start=True, stop=True),
              reads=[B.ones1f[0:1, 0:128], r_var], writes=[B.psStat[:, 0:512]])
        kb.op("tensor", lambda te: te.matmul(B.psStat[:, 512:1024], B.ones1f[0:1, 0:128], r_mean,
                                             start=True, stop=True),
              reads=[B.ones1f[0:1, 0:128], r_mean], writes=[B.psStat[:, 512:1024]])
        for c in range(NC):
            t1 = B.SC[:, 512:1024]
            kb.op("vector", lambda v, c=c, j=j, t1=t1: v.tensor_tensor(
                t1, src(c, j), B.psStat[:, 0:512], OP.mult),
                reads=[src(c, j), B.psStat[:, 0:512]], writes=[t1])
            fin = t1
            kb.op("vector", lambda v, c=c, j=j, fin=fin: v.tensor_sub(
                fin, fin, B.psStat[:, 512:1024]),
                reads=[fin, B.psStat[:, 512:1024]], writes=[fin])
            if g_ap is not None:
                kb.op("vector", lambda v, c=c, fin=fin: v.tensor_scalar(
                    fin, fin, g_ap[:, c:c + 1], b_ap[:, c:c + 1], OP.mult, OP.add),
                    reads=[fin, g_ap[:, c:c + 1], b_ap[:, c:c + 1]], writes=[fin])
            if dst_f32 is not None:
                kb.op("vector", lambda v, c=c, j=j, fin=fin: v.tensor_copy(dst_f32(c, j), fin),
                      reads=[fin], writes=[dst_f32(c, j)])
            if dst_bf is not None:
                kb.op("scalar", lambda sc, c=c, j=j, fin=fin: sc.activation(dst_bf(c, j), fin, AF.Copy),
                      reads=[fin], writes=[dst_bf(c, j)])


def ph_store_out(kb, B, s, src_buf):
    kb.op("sync", lambda sy: sy.dma_start(
        out=B.outT[s].rearrange("(c p) n -> p c n", p=128),
        in_=src_buf[:, :].rearrange("p (c n) -> p c n", c=NC)),
        reads=[src_buf[:, :]], writes=[B.outT[s]], dma=True)


def ph_dump(kb, B, src_ap, cast=False, col0=0):
    if B.dumpT is None:
        return
    w = 1
    for stp, cnt in src_ap.ap[1:]:
        w = max(w, cnt * stp)
    p = src_ap.ap[0][1]
    dst = B.dumpT[0:p, col0:col0 + w]
    kb.op("gpsimd", lambda e: e.dma_start(out=dst, in_=src_ap), reads=[src_ap], writes=[dst], dma=True)


def ph_finish(kb, B):
    kb.barrier_all("gpsimd")


def build(inputs_shapes_only=None, upto="all", dump=None, nseq=2, pf_stop=None, moe_stop=None, l1_stop=None):
    nc = bass.Bass("TRN2", target_bir_lowering=False, debug=False)
    B = declare(nc, dump=dump is not None)
    kb = KB(nc)
    kb.ro_names = {nm for nm, _ in SHARED_SHAPES} | {"xT"}
    ph_const_load(kb, B)
    for s in range(nseq):
        kb.settag(f"x{s}")
        ph_load_x(kb, B, s)
        ph_layernorm(kb, B, f"ln0_{s}", src=lambda c, j: fm(B.X, c, j),
                     dst_bf=lambda c, j: fm(B.XN, c, j))
        if upto == "ln0":
            if dump == "XN" and s == 0:
                ph_dump(kb, B, B.XN, cast=True)
            ph_store_out(kb, B, s, B.X)
            continue
        done = ph_performer(kb, B, s, pf_stop=pf_stop)
        if done:
            break
        if upto == "pf":
            ph_store_out(kb, B, s, B.X)
            continue
        done = ph_moe(kb, B, s, 0, moe_stop=moe_stop)
        if done:
            break
        if upto == "moe0":
            ph_store_out(kb, B, s, B.X)
            continue
        ph_spill_x(kb, B, s)
        ph_layernorm(kb, B, f"ln1_{s}", src=lambda c, j: fm(B.X, c, j),
                     dst_bf=lambda c, j: fm(B.XN, c, j),
                     dst_f32=lambda c, j: fm(B.X, c, j))
        done = ph_l1attn(kb, B, s, l1_stop=l1_stop)
        if done:
            if upto == "l1a":
                ph_store_out(kb, B, s, B.X)
            break
        if upto == "l1a":
            ph_store_out(kb, B, s, B.X)
            continue
        ph_unspill_add(kb, B, s)
        if upto == "l1x":
            ph_store_out(kb, B, s, B.X)
            continue
        ph_moe(kb, B, s, 1)
        ph_store_out(kb, B, s, B.X)
    ph_finish(kb, B)
    n, ninc = kb.finalize()
    return nc, B, (n, ninc)



def ph_performer(kb, B, s, pf_stop=None):
    t = f"pf{s}"
    kb.settag(t)
    def stop(name):
        if pf_stop == name:
            if name in ("a1", "b2"):
                ph_dump(kb, B, B.TA[:, 0:8192], cast=True)
            elif name == "a2":
                ph_dump(kb, B, B.NK[:, :], cast=False)
            elif name == "a3":
                ph_dump(kb, B, B.TC[:, 0:8192], cast=True)
            elif name == "a4":
                ph_dump(kb, B, B.TA[:, 8192:16384], cast=True, col0=8192)
            elif name == "a5":
                ph_dump(kb, B, B.KV[:, :], cast=True)
                ph_dump(kb, B, B.KSA[:, :], cast=True, col0=1024)
            elif name == "b3":
                ph_dump(kb, B, B.TA[:, 8192:16384], cast=True, col0=8192)
            elif name == "b5":
                ph_dump(kb, B, B.TC[:, 0:8192], cast=True)
            return True
        return False
    WT0 = B.WT[:, 0:8192]
    WT1 = B.WT[:, 8192:16384]
    wq = B.wqkvT0

    def wload(dst, collo):
        kb.op("gpsimd", lambda g, dst=dst, collo=collo: g.dma_start(
            out=dst.rearrange("p (c m) -> p c m", c=NC),
            in_=wq[:, collo:collo + 1024].rearrange("(c p) m -> p c m", p=128)),
            reads=[], writes=[dst], dma=True)

    wload(WT0, 1024)
    wload(WT1, 2048)
    if pf_stop == "w0":
        ph_dump(kb, B, B.WT[:, 0:8192], cast=True)
        return True
    if pf_stop == "xn":
        ph_dump(kb, B, B.XN[:, 0:8192], cast=True)
        return True

    def kfm(c, hj):
        return B.TA[:, c * 1024 + hj * 512:c * 1024 + (hj + 1) * 512]

    def vtm(tt):
        return B.TA[:, 8192 + tt * 1024:8192 + (tt + 1) * 1024]

    def kptm(tt):
        return B.TC[:, tt * 1024:(tt + 1) * 1024]

    for bank in range(3):
        kb.op("tensor", lambda te, bank=bank: te.matmul(
            B.psS[:, bank * 512:(bank + 1) * 512], B.z128[0:1, 0:128],
            B.EcS[0:1, 0:512], start=True, stop=False, skip_group_check=True),
            reads=[B.z128[:, :], B.EcS[0:1, :]], writes=[B.psS[:, bank * 512:(bank + 1) * 512]])

    import os
    _ng = int(os.environ.get("A1_NGROUPS", "0"))
    _noc = (_ng + 1) // 2 if _ng else NC
    for h in range(1 if pf_stop in ("a1x",) or _ng else 2):
        for oc in range(1 if pf_stop == "a1x" else _noc if _ng else NC):
            for hj in range(1 if pf_stop == "a1x" else 2):
                if _ng and oc * 2 + hj >= _ng:
                    continue
                ps = B.psA if (oc * 2 + hj) % 2 == 0 else B.psB
                for kc in range(NC):
                    kb.op("tensor", lambda te, ps=ps, oc=oc, hj=hj, kc=kc, h=h: te.matmul(
                        ps[:, :], B.WT[:, kc * 1024 + oc * 128:kc * 1024 + oc * 128 + 128],
                        fm(B.XN, kc, 2 * h + hj), start=(kc == 0), stop=(kc == NC - 1)),
                        reads=[WT0, fm(B.XN, kc, 2 * h + hj)], writes=[ps[:, :]])
                kb.op("vector", lambda v, ps=ps, oc=oc, hj=hj: v.tensor_copy(kfm(oc, hj), ps[:, :]),
                      reads=[ps[:, :]], writes=[kfm(oc, hj)])
        if pf_stop == "a1x":
            ph_dump(kb, B, B.TA[:, 0:8192], cast=True)
            return True
        if stop("a1"): return
        for hj in range(2):
            pst = B.psStat[0:16, hj * 512:(hj + 1) * 512]
            for kc in range(NC):
                sq = B.SB[:, 0:512]
                kb.op("scalar", lambda sc, kc=kc, hj=hj, sq=sq: sc.activation(sq, kfm(kc, hj), AF.Square),
                      reads=[kfm(kc, hj)], writes=[sq])
                kb.op("tensor", lambda te, sq=sq, kc=kc, pst=pst: te.matmul(
                    pst, B.E2S[:, kc * 16:(kc + 1) * 16], sq, start=(kc == 0), stop=(kc == NC - 1)),
                    reads=[B.E2S[:, :], sq], writes=[pst])
            nr = B.SR[0:16, 0:512]
            kb.op("scalar", lambda sc, pst=pst, nr=nr: sc.activation(nr, pst, AF.Ln),
                  reads=[pst], writes=[nr])
            kb.op("scalar", lambda sc, nr=nr: sc.activation(nr, nr, AF.Exp, scale=0.5),
                  reads=[nr], writes=[nr])
            kb.op("vector", lambda v, nr=nr: v.tensor_scalar(nr, nr, -1.0, None, OP.mult),
                  reads=[nr], writes=[nr])
            for q in range(4):
                tt = hj * 4 + q
                gt = h * 8 + tt
                kb.op("tensor", lambda te, nr=nr, q=q: te.transpose(
                    B.psB[:, 0:16], nr[:, q * 128:(q + 1) * 128], B.idn[0:16, 0:16]),
                    reads=[nr, B.idn[:, :]], writes=[B.psB[:, 0:16]])
                kb.op("vector", lambda v, gt=gt: v.tensor_copy(
                    B.NK[:, gt * 16:(gt + 1) * 16], B.psB[:, 0:16]),
                    reads=[B.psB[:, 0:16]], writes=[B.NK[:, gt * 16:(gt + 1) * 16]])
        if stop("a2"): return
        for tt in range(8):
            gt = h * 8 + tt
            hj, q = tt // 4, tt % 4
            for p in range(NC):
                ps = B.psA if p < 4 else B.psB
                col = (p % 4) * 128
                lhsT = B.TA[:, p * 1024 + hj * 512 + q * 128:p * 1024 + hj * 512 + q * 128 + 128]
                kb.op("tensor", lambda te, lhsT=lhsT, ps=ps, col=col: te.matmul(
                    ps[:, col:col + 128], lhsT, B.bdrfS[:, :], start=True, stop=True),
                    reads=[lhsT, B.bdrfS[:, :]], writes=[ps[:, col:col + 128]])
            for hh in range(HEADS):
                p = hh // 2
                ps = B.psA if p < 4 else B.psB
                col = (p % 4) * 128 + (hh % 2) * 64
                kb.op("scalar", lambda sc, ps=ps, col=col, tt=tt, gt=gt, hh=hh: sc.activation(
                    kptm(tt)[:, hh * 64:hh * 64 + 64], ps[:, col:col + 64], AF.Exp,
                    bias=B.NK[:, gt * 16 + hh:gt * 16 + hh + 1], scale=0.125),
                    reads=[ps[:, col:col + 64], B.NK[:, gt * 16 + hh:gt * 16 + hh + 1]],
                    writes=[kptm(tt)[:, hh * 64:hh * 64 + 64]])
        if stop("a3"): return
        for tt in range(8):
            hj, q = tt // 4, tt % 4
            for vh in range(2):
                pv = B.psStat[:, vh * 512:(vh + 1) * 512]
                for kc in range(NC):
                    lhsT = fm(B.XN, kc, 2 * h + hj)[:, q * 128:(q + 1) * 128]
                    rhs = B.WT[:, 8192 + kc * 1024 + vh * 512:8192 + kc * 1024 + (vh + 1) * 512]
                    kb.op("tensor", lambda te, lhsT=lhsT, rhs=rhs, pv=pv, kc=kc: te.matmul(
                        pv, lhsT, rhs, start=(kc == 0), stop=(kc == NC - 1)),
                        reads=[lhsT, WT1], writes=[pv])
                kb.op("vector", lambda v, pv=pv, tt=tt, vh=vh: v.tensor_copy(
                    vtm(tt)[:, vh * 512:(vh + 1) * 512], pv),
                    reads=[pv], writes=[vtm(tt)[:, vh * 512:(vh + 1) * 512]])
        if stop("a4"): return
        for tt in range(8):
            last = (h == 1 and tt == 7)
            for p in range(NC):
                kb.op("tensor", lambda te, tt=tt, p=p, last=last: te.matmul(
                    B.psS[:, p * 128:(p + 1) * 128], kptm(tt)[:, p * 128:(p + 1) * 128],
                    vtm(tt)[:, p * 128:(p + 1) * 128], start=False, stop=last,
                    skip_group_check=True),
                    reads=[kptm(tt)[:, p * 128:(p + 1) * 128], vtm(tt)[:, p * 128:(p + 1) * 128]],
                    writes=[B.psS[:, p * 128:(p + 1) * 128]])
                kb.op("tensor", lambda te, tt=tt, p=p, last=last: te.matmul(
                    B.psS[:, 1024 + p:1024 + p + 1], kptm(tt)[:, p * 128:(p + 1) * 128],
                    B.onesc[:, 0:1], start=False, stop=last, skip_group_check=True),
                    reads=[kptm(tt)[:, p * 128:(p + 1) * 128], B.onesc[:, :]],
                    writes=[B.psS[:, 1024 + p:1024 + p + 1]])
    wload(WT0, 0)
    kb.op("gpsimd", lambda g: g.dma_start(
        out=B.WT[:, 8192:16384].rearrange("p (c m) -> p c m", c=NC),
        in_=B.woutT0[:, :].rearrange("(c p) m -> p c m", p=128)),
        reads=[], writes=[B.WT[:, 8192:16384]], dma=True)
    kb.op("vector", lambda v: v.memset(B.KSA[:, :], 0.0), writes=[B.KSA[:, :]])
    kb.op("vector", lambda v: v.memset(B.KV[:, :], 0.0), writes=[B.KV[:, :]])
    for c in range(NC):
        kb.op("vector", lambda v, c=c: v.tensor_copy(
            B.KSA[0:64, c * 16 + 2 * c:c * 16 + 2 * c + 1], B.psS[0:64, 1024 + c:1024 + c + 1]),
            reads=[B.psS[:, 1024:1036]], writes=[B.KSA[:, c * 16:(c + 1) * 16]])
        kb.op("vector", lambda v, c=c: v.tensor_copy(
            B.KSA[64:128, c * 16 + 2 * c + 1:c * 16 + 2 * c + 2], B.psS[64:128, 1024 + c:1024 + c + 1]),
            reads=[B.psS[:, 1024:1036]], writes=[B.KSA[:, c * 16:(c + 1) * 16]])
        kb.op("vector", lambda v, c=c: v.tensor_copy(
            B.KV[0:64, c * 128:c * 128 + 64], B.psS[0:64, c * 128:c * 128 + 64]),
            reads=[B.psS[:, c * 128:(c + 1) * 128]], writes=[B.KV[:, c * 128:(c + 1) * 128]])
        kb.op("vector", lambda v, c=c: v.tensor_copy(
            B.KV[64:128, c * 128 + 64:c * 128 + 128], B.psS[64:128, c * 128 + 64:c * 128 + 128]),
            reads=[B.psS[:, c * 128:(c + 1) * 128]], writes=[B.KV[:, c * 128:(c + 1) * 128]])
    if stop("a5"): return
    for h in range(2):
        for oc in range(NC):
            for hj in range(2):
                ps = B.psA if (oc * 2 + hj) % 2 == 0 else B.psB
                for kc in range(NC):
                    kb.op("tensor", lambda te, ps=ps, oc=oc, hj=hj, kc=kc, h=h: te.matmul(
                        ps[:, :], B.WT[:, kc * 1024 + oc * 128:kc * 1024 + oc * 128 + 128],
                        fm(B.XN, kc, 2 * h + hj), start=(kc == 0), stop=(kc == NC - 1)),
                        reads=[WT0, fm(B.XN, kc, 2 * h + hj)], writes=[ps[:, :]])
                kb.op("vector", lambda v, ps=ps, oc=oc, hj=hj: v.tensor_copy(kfm(oc, hj), ps[:, :]),
                      reads=[ps[:, :]], writes=[kfm(oc, hj)])
        if stop("b2"): return
        for hj in range(2):
            pst = B.psStat[0:16, 0:512]
            for kc in range(NC):
                sq = B.SB[:, 0:512]
                kb.op("scalar", lambda sc, kc=kc, hj=hj, sq=sq: sc.activation(sq, kfm(kc, hj), AF.Square),
                      reads=[kfm(kc, hj)], writes=[sq])
                kb.op("tensor", lambda te, sq=sq, kc=kc, pst=pst: te.matmul(
                    pst, B.E2S[:, kc * 16:(kc + 1) * 16], sq, start=(kc == 0), stop=(kc == NC - 1)),
                    reads=[B.E2S[:, :], sq], writes=[pst])
            nqს = B.SR[0:16, 0:512]
            kb.op("scalar", lambda sc, pst=pst, nq=nqს: sc.activation(nq, pst, AF.Ln),
                  reads=[pst], writes=[nqს])
            kb.op("scalar", lambda sc, nq=nqს: sc.activation(nq, nq, AF.Exp, scale=0.5),
                  reads=[nqს], writes=[nqს])
            kb.op("vector", lambda v, nq=nqს: v.tensor_scalar(nq, nq, -1.0, None, OP.mult),
                  reads=[nqს], writes=[nqს])
            kb.op("scalar", lambda sc, nq=nqს: sc.activation(B.NQ[:, :], nq, AF.Exp),
                  reads=[nqს], writes=[B.NQ[:, :]])
            if pf_stop == "b2n":
                ph_dump(kb, B, B.NQ[:, :], cast=False)
                return True
            for c in range(NC):
                kb.op("tensor", lambda te, c=c, hj=hj, h=h: te.matmul(
                    B.psA[:, :], B.bdrfS[:, :], kfm(c, hj), start=True, stop=True),
                    reads=[B.bdrfS[:, :], kfm(c, hj)], writes=[B.psA[:, :]])
                e = B.SC[:, 0:512]
                kb.op("scalar", lambda sc, e=e: sc.activation(e, B.psA[:, :], AF.Exp, scale=0.125),
                      reads=[B.psA[:, :]], writes=[e])
                kb.op("tensor", lambda te, c=c: te.matmul(
                    B.psB[:, :], B.EcS[:, c * 128:(c + 1) * 128], B.NQ[:, :], start=True, stop=True),
                    reads=[B.EcS[:, :], B.NQ[:, :]], writes=[B.psB[:, :]])
                qp = B.TA[:, 8192 + c * 1024 + hj * 512:8192 + c * 1024 + (hj + 1) * 512]
                kb.op("vector", lambda v, qp=qp, e=e: v.tensor_tensor(qp, e, B.psB[:, :], OP.mult),
                      reads=[e, B.psB[:, :]], writes=[qp])
            if stop("b3"): return
            pno = B.psStat[0:16, 512:1024]
            for c in range(NC):
                qp = B.TA[:, 8192 + c * 1024 + hj * 512:8192 + c * 1024 + (hj + 1) * 512]
                kb.op("tensor", lambda te, c=c, qp=qp, pno=pno: te.matmul(
                    pno, B.KSA[:, c * 16:(c + 1) * 16], qp, start=(c == 0), stop=(c == NC - 1)),
                    reads=[B.KSA[:, :], qp], writes=[pno])
            rns = B.SR[0:16, 0:512]
            kb.op("vector", lambda v, pno=pno, rns=rns: v.tensor_scalar(rns, pno, 1e-6, None, OP.add),
                  reads=[pno], writes=[rns])
            kb.op("vector", lambda v, rns=rns: v.reciprocal(B.RN[:, :], rns),
                  reads=[rns], writes=[B.RN[:, :]])
            for c in range(NC):
                qp = B.TA[:, 8192 + c * 1024 + hj * 512:8192 + c * 1024 + (hj + 1) * 512]
                kb.op("tensor", lambda te, c=c, qp=qp: te.matmul(
                    B.psA[:, :], B.KV[:, c * 128:(c + 1) * 128], qp, start=True, stop=True),
                    reads=[B.KV[:, :], qp], writes=[B.psA[:, :]])
                kb.op("tensor", lambda te, c=c: te.matmul(
                    B.psB[:, :], B.EcS[:, c * 128:(c + 1) * 128], B.RN[:, :], start=True, stop=True),
                    reads=[B.EcS[:, :], B.RN[:, :]], writes=[B.psB[:, :]])
                rb = B.SC[:, 512:1024]
                kb.op("scalar", lambda sc, rb=rb: sc.activation(rb, B.psB[:, :], AF.Copy),
                      reads=[B.psB[:, :]], writes=[rb])
                mg = B.TC[:, c * 1024 + hj * 512:c * 1024 + (hj + 1) * 512]
                kb.op("vector", lambda v, mg=mg, rb=rb: v.tensor_tensor(mg, B.psA[:, :], rb, OP.mult),
                      reads=[B.psA[:, :], rb], writes=[mg])
        if stop("b5"): return
        for oc in range(NC):
            for hj in range(2):
                j = 2 * h + hj
                ps = B.psA if (oc * 2 + hj) % 2 == 0 else B.psB
                for kc in range(NC):
                    mg = B.TC[:, kc * 1024 + hj * 512:kc * 1024 + (hj + 1) * 512]
                    kb.op("tensor", lambda te, ps=ps, oc=oc, kc=kc, mg=mg: te.matmul(
                        ps[:, :], B.WT[:, 8192 + kc * 1024 + oc * 128:8192 + kc * 1024 + oc * 128 + 128],
                        mg, start=(kc == 0), stop=(kc == NC - 1)),
                        reads=[B.WT[:, 8192:16384], mg], writes=[ps[:, :]])
                kb.op("vector", lambda v, ps=ps, oc=oc, j=j: v.scalar_tensor_tensor(
                    fm(B.X, oc, j), ps[:, :], B.BIAS[:, oc:oc + 1], fm(B.X, oc, j), OP.add, OP.add),
                    reads=[ps[:, :], B.BIAS[:, 0:8], fm(B.X, oc, j)], writes=[fm(B.X, oc, j)])



def ph_moe(kb, B, s, layer, moe_stop=None):
    t = f"mo{layer}_{s}"
    kb.settag(t)
    if layer == 0:
        gate_h, w1h, w2h, w3h, NSL = B.wr0, B.w1_0, B.w2_0, B.w3_0, 3
    else:
        gate_h, w1h, w2h, w3h, NSL = B.gate1, B.ew1, B.ew2, B.ew3, 8
    GT = B.TC[:, 6144:6208]
    kb.op("gpsimd", lambda g: g.dma_start(
        out=GT.rearrange("p (c e) -> p c e", c=NC),
        in_=gate_h[:, :].rearrange("(c p) e -> p c e", p=128)),
        reads=[], writes=[GT], dma=True)
    for tt in range(NT1):
        j, q = tt // 4, tt % 4
        pl = B.psStat[:, 0:8]
        for kc in range(NC):
            lhsT = fm(B.XN, kc, j)[:, q * 128:(q + 1) * 128]
            kb.op("tensor", lambda te, lhsT=lhsT, kc=kc, pl=pl: te.matmul(
                pl, lhsT, GT[:, kc * 8:(kc + 1) * 8], start=(kc == 0), stop=(kc == NC - 1)),
                reads=[lhsT, GT], writes=[pl])
        _gc = os.environ.get("GATE_CUT", "all")
        if _gc == "logits":
            continue
        m1, m2, d_, ed, s1, r_, w2_ = (B.SR[:, 8:9], B.SR[:, 9:10], B.SR[:, 10:11],
                                       B.SR[:, 11:12], B.SR[:, 12:13], B.SR[:, 13:14],
                                       B.SR[:, 14:15])
        eq1, l2, eq2, c1, comb = (B.SR[:, 16:24], B.SR[:, 24:32], B.SR[:, 40:48],
                                  B.SR[:, 48:56], B.SR[:, 56:64])
        pls = B.SR[:, 0:8]
        kb.op("vector", lambda v, pl=pl: v.tensor_copy(pls, pl), reads=[pl], writes=[pls])
        kb.op("vector", lambda v: v.reduce_max(m1, pls, axis=AX), reads=[pls], writes=[m1])
        kb.op("vector", lambda v: v.tensor_scalar(eq1, pls, m1[:, 0:1], None, OP.is_equal),
              reads=[pls, m1], writes=[eq1])
        kb.op("vector", lambda v: v.scalar_tensor_tensor(l2, eq1, -1e30, pls, OP.mult, OP.add),
              reads=[eq1, pls], writes=[l2])
        kb.op("vector", lambda v: v.reduce_max(m2, l2, axis=AX), reads=[l2], writes=[m2])
        kb.op("vector", lambda v: v.tensor_sub(d_, m2, m1), reads=[m2, m1], writes=[d_])
        kb.op("scalar", lambda sc: sc.activation(ed, d_, AF.Exp), reads=[d_], writes=[ed])
        kb.op("vector", lambda v: v.tensor_scalar(s1, ed, 1.0, None, OP.add), reads=[ed], writes=[s1])
        kb.op("vector", lambda v: v.reciprocal(r_, s1), reads=[s1], writes=[r_])
        kb.op("vector", lambda v: v.tensor_mul(w2_, ed, r_), reads=[ed, r_], writes=[w2_])
        kb.op("vector", lambda v: v.tensor_scalar(eq2, l2, m2[:, 0:1], None, OP.is_equal),
              reads=[l2, m2], writes=[eq2])
        kb.op("vector", lambda v: v.tensor_scalar(c1, eq1, r_[:, 0:1], None, OP.mult),
              reads=[eq1, r_], writes=[c1])
        kb.op("vector", lambda v: v.scalar_tensor_tensor(comb, eq2, w2_[:, 0:1], c1, OP.mult, OP.add),
              reads=[eq2, w2_, c1], writes=[comb])
        if _gc == "chain":
            continue
        if moe_stop == "gate" and tt == 0:
            kb.op("vector", lambda v, pl=pl: v.tensor_copy(B.SR[:, 64:72], pl),
                  reads=[pl], writes=[B.SR[:, 64:72]])
        kb.op("tensor", lambda te: te.transpose(B.psB[0:8, 0:128], comb, B.idn[:, :]),
              reads=[comb, B.idn[:, :]], writes=[B.psB[:, 0:128]])
        if _gc == "trans":
            continue
        kb.op("vector", lambda v, tt=tt: v.tensor_copy(B.CMB[0:8, tt * 128:(tt + 1) * 128],
                                                       B.psB[0:8, 0:128]),
              reads=[B.psB[:, 0:128]], writes=[B.CMB[0:8, tt * 128:(tt + 1) * 128]])
    if moe_stop == "gate":
        ph_dump(kb, B, B.CMB[:, :], cast=True)
        ph_dump(kb, B, B.SR[:, 0:128], cast=False, col0=4096)
        return True
    if moe_stop == "noex":
        ph_dump(kb, B, B.X[:, :], cast=False)
        return True
    _ne = 1 if moe_stop == "exp0" else 8
    _nsl = 1 if moe_stop == "exp0" else NSL
    for e in range(_ne):
        for sl in range(_nsl):
            par = (e * NSL + sl) % 2
            WB = B.WT[:, 0:12288] if par == 0 else B.TA[:, 0:12288]
            kb.op("gpsimd", lambda g, e=e, sl=sl, WB=WB: g.dma_start(
                out=WB[:, 0:4096].rearrange("p (c m) -> p c m", c=NC),
                in_=w1h[e, :, sl * 512:(sl + 1) * 512].rearrange("(c p) m -> p c m", p=128)),
                reads=[], writes=[WB[:, 0:4096]], dma=True)
            kb.op("gpsimd", lambda g, e=e, sl=sl, WB=WB: g.dma_start(
                out=WB[:, 4096:8192].rearrange("p (c m) -> p c m", c=NC),
                in_=w3h[e, :, sl * 512:(sl + 1) * 512].rearrange("(c p) m -> p c m", p=128)),
                reads=[], writes=[WB[:, 4096:8192]], dma=True)
            kb.op("gpsimd", lambda g, e=e, sl=sl, WB=WB: g.dma_start(
                out=WB[:, 8192:12288].rearrange("p (c m) -> p c m", c=4),
                in_=w2h[e, sl * 512:(sl + 1) * 512, :].rearrange("(c p) m -> p c m", p=128)),
                reads=[], writes=[WB[:, 8192:12288]], dma=True)
            for j in range(NT5):
                cbc = B.psS[:, 0:512]
                if not os.environ.get("MOE_NOCBC"):
                    kb.op("tensor", lambda te, e=e, j=j, cbc=cbc: te.matmul(
                        cbc, B.E8S[0:8, e * 128:(e + 1) * 128], B.CMB[0:8, j * 512:(j + 1) * 512],
                        start=True, stop=True),
                        reads=[B.E8S[:, :], B.CMB[0:8, j * 512:(j + 1) * 512]], writes=[cbc])
                    for kc in range(NC):
                        xc = B.TC[:, kc * 512:(kc + 1) * 512]
                        kb.op("vector", lambda v, kc=kc, j=j, xc=xc, cbc=cbc: v.tensor_tensor(
                            xc, fm(B.XN, kc, j), cbc, OP.mult),
                            reads=[fm(B.XN, kc, j), cbc], writes=[xc])
                else:
                    for kc in range(NC):
                        xc = B.TC[:, kc * 512:(kc + 1) * 512]
                        kb.op("vector", lambda v, kc=kc, j=j, xc=xc: v.tensor_copy(
                            xc, fm(B.XN, kc, j)),
                            reads=[fm(B.XN, kc, j)], writes=[xc])
                if moe_stop == "xnc" and e == 0 and sl == 0 and j == 0:
                    ph_dump(kb, B, B.TC[:, 0:4096], cast=True)
                    ph_dump(kb, B, B.CMB[:, :], cast=True, col0=8192)
                    return True
                for hc in range(4):
                    for kc in range(NC):
                        kb.op("tensor", lambda te, WB=WB, kc=kc, hc=hc, j=j: te.matmul(
                            B.psA[:, :], WB[:, kc * 512 + hc * 128:kc * 512 + hc * 128 + 128],
                            fm(B.XN, kc, j), start=(kc == 0), stop=(kc == NC - 1)),
                            reads=[WB[:, 0:4096], fm(B.XN, kc, j)], writes=[B.psA[:, :]])
                    for kc in range(NC):
                        xc = B.TC[:, kc * 512:(kc + 1) * 512]
                        kb.op("tensor", lambda te, WB=WB, kc=kc, hc=hc, xc=xc: te.matmul(
                            B.psB[:, :], WB[:, 4096 + kc * 512 + hc * 128:4096 + kc * 512 + hc * 128 + 128],
                            xc, start=(kc == 0), stop=(kc == NC - 1)),
                            reads=[WB[:, 4096:8192], xc], writes=[B.psB[:, :]])
                    sil = B.SC[:, 512:1024]
                    if moe_stop == "h0raw" and e == 0 and sl == 0 and j == 0 and hc == 0:
                        if os.environ.get("MEMSET7"):
                            kb.op("vector", lambda v: v.memset(B.SC[:, 0:512], 7.0),
                                  writes=[B.SC[:, 0:512]])
                        else:
                            kb.op("vector", lambda v: v.tensor_copy(B.SC[:, 0:512], B.psA[:, :]),
                                  reads=[B.psA[:, :]] + ([B.psB[:, :]] if os.environ.get("WAITG") else []),
                                  writes=[B.SC[:, 0:512]])
                        ph_dump(kb, B, B.SC[:, 0:512], cast=False)
                        return True
                    kb.op("scalar", lambda sc, sil=sil: sc.activation(sil, B.psA[:, :], AF.Silu),
                          reads=[B.psA[:, :]], writes=[sil])
                    if moe_stop == "h0" and e == 0 and sl == 0 and j == 0 and hc == 0:
                        ph_dump(kb, B, sil, cast=False)
                        kb.op("vector", lambda v: v.tensor_copy(B.SC[:, 0:512], B.psB[:, :]),
                              reads=[B.psB[:, :]], writes=[B.SC[:, 0:512]])
                        ph_dump(kb, B, B.SC[:, 0:512], cast=False, col0=512)
                        return True
                    hsl = B.TC[:, 4096 + hc * 512:4096 + (hc + 1) * 512]
                    kb.op("vector", lambda v, hsl=hsl, sil=sil: v.tensor_tensor(
                        hsl, sil, B.psB[:, :], OP.mult),
                        reads=[sil, B.psB[:, :]], writes=[hsl])
                for oc in range(NC):
                    yp = B.psS[:, 512:1024]
                    for hc in range(4):
                        hsl = B.TC[:, 4096 + hc * 512:4096 + (hc + 1) * 512]
                        kb.op("tensor", lambda te, WB=WB, hc=hc, oc=oc, hsl=hsl: te.matmul(
                            yp, WB[:, 8192 + hc * 1024 + oc * 128:8192 + hc * 1024 + oc * 128 + 128],
                            hsl, start=(hc == 0), stop=(hc == 3)),
                            reads=[WB[:, 8192:12288], hsl], writes=[yp])
                    kb.op("vector", lambda v, oc=oc, j=j: v.tensor_tensor(
                        fm(B.X, oc, j), yp, fm(B.X, oc, j), OP.add),
                        reads=[yp, fm(B.X, oc, j)], writes=[fm(B.X, oc, j)])
    if moe_stop == "exp0":
        ph_store_out(kb, B, s, B.X)
        return True



def ph_l1attn(kb, B, s, l1_stop=None):
    t = f"l1a{s}"
    kb.settag(t)
    H = B.X

    QH = B.TC[0:64, 0:2048]
    KHt = B.TC[0:64, 2048:4096]
    VH = B.TC[:, 4096:5120]
    OUTH = B.TC[0:64, 5120:7168]
    PQ = B.WT[:, 12288:14336]
    PTb = B.WT[:, 14336:16384]
    wqT = B.WT[:, 0:512]
    wkT = B.WT[:, 512:1024]
    wvT = B.WT[:, 1024:1536]
    ENCC = B.WT[:, 3584:5632]

    def attn_sublayer(kind):
        nonlocal l1_stop
        lnmap = {"self": 0, "cross": 2}[kind]
        ph_layernorm(kb, B, f"{t}{kind}ln", src=lambda c, j: fm(H, c, j),
                     dst_bf=lambda c, j: fm(B.TA, c, j),
                     g_ap=B.LNP[:, lnmap * NC:(lnmap + 1) * NC],
                     b_ap=B.LNP[:, (lnmap + 1) * NC:(lnmap + 2) * NC])
        kb.settag(t + kind)
        vbh = B.cattnbV if kind == "self" else B.xkvbV
        kb.op("gpsimd", lambda g, vbh=vbh: g.dma_start(out=B.VBR[0:1, :], in_=vbh[:, :]),
              reads=[], writes=[B.VBR[0:1, :]], dma=True)
        pw = B.aprojw if kind == "self" else B.xprojw
        bslot = 32 if kind == "self" else 64
        for hh in range(HEADS):
            if kind == "self":
                qsrc, qcol = B.cattnw, hh * 64
                ksrc_w, kcol = B.cattnw, 1024 + hh * 64
                vsrc_w, vcol = B.cattnw, 2048 + hh * 64
                qb_col, kb_col, vb_col = 8 + hh // 2, 16 + hh // 2, hh * 64
            else:
                qsrc, qcol = B.qattnw, hh * 64
                ksrc_w, kcol = B.xkvw, hh * 64
                vsrc_w, vcol = B.xkvw, 1024 + hh * 64
                qb_col, kb_col, vb_col = 40 + hh // 2, 48 + hh // 2, hh * 64
            for dst, wsrc, c0 in ((wqT, qsrc, qcol), (wkT, ksrc_w, kcol), (wvT, vsrc_w, vcol)):
                kb.op("gpsimd", lambda g, dst=dst, wsrc=wsrc, c0=c0: g.dma_start(
                    out=dst.rearrange("p (c m) -> p c m", c=NC),
                    in_=wsrc[:, c0:c0 + 64].rearrange("(c p) m -> p c m", p=128)),
                    reads=[], writes=[dst], dma=True)
            qb_ap = B.BIAS[(hh * 64) % 128:(hh * 64) % 128 + 64, qb_col:qb_col + 1]
            kb_ap = B.BIAS[(hh * 64) % 128:(hh * 64) % 128 + 64, kb_col:kb_col + 1]
            for jt in range(NT5):
                for kc in range(NC):
                    kb.op("tensor", lambda te, kc=kc, jt=jt: te.matmul(
                        B.psA[0:64, :], B.WT[:, kc * 64:kc * 64 + 64],
                        fm(B.TA, kc, jt), start=(kc == 0), stop=(kc == NC - 1)),
                        reads=[wqT, fm(B.TA, kc, jt)], writes=[B.psA[:, 0:512]])
                kb.op("scalar", lambda sc, jt=jt, qb_ap=qb_ap: sc.activation(
                    QH[:, jt * 512:(jt + 1) * 512], B.psA[0:64, :], AF.Identity, bias=qb_ap),
                    reads=[B.psA[:, 0:512], qb_ap], writes=[QH[:, jt * 512:(jt + 1) * 512]])
            if kind == "self":
                for jt in range(NT5):
                    for kc in range(NC):
                        kb.op("tensor", lambda te, kc=kc, jt=jt: te.matmul(
                            B.psB[0:64, :], B.WT[:, 512 + kc * 64:512 + kc * 64 + 64],
                            fm(B.TA, kc, jt), start=(kc == 0), stop=(kc == NC - 1)),
                            reads=[wkT, fm(B.TA, kc, jt)], writes=[B.psB[:, 0:512]])
                    kb.op("scalar", lambda sc, jt=jt, kb_ap=kb_ap: sc.activation(
                        KHt[:, jt * 512:(jt + 1) * 512], B.psB[0:64, :], AF.Identity, bias=kb_ap),
                        reads=[B.psB[:, 0:512], kb_ap], writes=[KHt[:, jt * 512:(jt + 1) * 512]])
                for tt in range(NT1):
                    pv = B.psStat[:, (tt % 8) * 64:(tt % 8) * 64 + 64]
                    kb.op("tensor", lambda te, pv=pv, vb_col=vb_col: te.matmul(
                        pv, B.E8S[0:1, 0:128], B.VBR[0:1, vb_col:vb_col + 64],
                        start=True, stop=False, skip_group_check=True),
                        reads=[B.E8S[:, :], B.VBR[0:1, 0:1024]], writes=[pv])
                    for kc in range(NC):
                        lhsT = fm(B.TA, kc, tt // 4)[:, (tt % 4) * 128:(tt % 4) * 128 + 128]
                        kb.op("tensor", lambda te, pv=pv, lhsT=lhsT, kc=kc: te.matmul(
                            pv, lhsT, B.WT[:, 1024 + kc * 64:1024 + kc * 64 + 64],
                            start=False, stop=(kc == NC - 1), skip_group_check=True),
                            reads=[lhsT, wvT], writes=[pv])
                    kb.op("vector", lambda v, tt=tt, pv=pv: v.tensor_copy(
                        VH[:, tt * 64:(tt + 1) * 64], pv),
                        reads=[pv], writes=[VH[:, tt * 64:(tt + 1) * 64]])
            else:
                vbb = B.SC[:, 768:832]
                kb.op("tensor", lambda te, vb_col=vb_col: te.matmul(
                    B.psB[:, 384:448], B.E8S[0:1, 0:128], B.VBR[0:1, vb_col:vb_col + 64],
                    start=True, stop=True),
                    reads=[B.E8S[:, :], B.VBR[0:1, 0:1024]], writes=[B.psB[:, 384:448]])
                kb.op("scalar", lambda sc, vbb=vbb: sc.activation(vbb, B.psB[:, 384:448], AF.Copy),
                      reads=[B.psB[:, 384:448]], writes=[vbb])
                for kc in range(NC):
                    kb.op("gpsimd", lambda g, kc=kc: g.dma_start(
                        out=ENCC, in_=B.encT[kc * 128:(kc + 1) * 128, :]),
                        reads=[], writes=[ENCC], dma=True)
                    for jt in range(NT5):
                        kb.op("tensor", lambda te, kc=kc, jt=jt: te.matmul(
                            B.psS[0:64, jt * 512:(jt + 1) * 512],
                            B.WT[:, 512 + kc * 64:512 + kc * 64 + 64],
                            ENCC[:, jt * 512:(jt + 1) * 512],
                            start=(kc == 0), stop=(kc == NC - 1), skip_group_check=True),
                            reads=[wkT, ENCC], writes=[B.psS[0:64, jt * 512:(jt + 1) * 512]])
                    for tt in range(NT1):
                        pv = B.psStat[:, tt * 64 % 1024:tt * 64 % 1024 + 64]
                        lhsT = ENCC[:, tt * 128:(tt + 1) * 128]
                        kb.op("tensor", lambda te, pv=pv, lhsT=lhsT, kc=kc: te.matmul(
                            pv, lhsT, B.WT[:, 1024 + kc * 64:1024 + kc * 64 + 64],
                            start=(kc == 0), stop=(kc == NC - 1), skip_group_check=True),
                            reads=[lhsT, wvT], writes=[pv])
                for jt in range(NT5):
                    kb.op("scalar", lambda sc, jt=jt, kb_ap=kb_ap: sc.activation(
                        KHt[:, jt * 512:(jt + 1) * 512], B.psS[0:64, jt * 512:(jt + 1) * 512],
                        AF.Identity, bias=kb_ap),
                        reads=[B.psS[0:64, jt * 512:(jt + 1) * 512], kb_ap],
                        writes=[KHt[:, jt * 512:(jt + 1) * 512]])
                for tt in range(NT1):
                    pv = B.psStat[:, tt * 64 % 1024:tt * 64 % 1024 + 64]
                    tsc = B.SC[:, 832:896]
                    kb.op("vector", lambda v, pv=pv, vbb=vbb, tsc=tsc: v.tensor_tensor(
                        tsc, pv, vbb, OP.add),
                        reads=[pv, vbb], writes=[tsc])
                    kb.op("vector", lambda v, tt=tt, tsc=tsc: v.tensor_scalar(
                        VH[:, tt * 64:(tt + 1) * 64], tsc, B.scorerS[:, tt:tt + 1], None, OP.mult),
                        reads=[tsc, B.scorerS[:, :]], writes=[VH[:, tt * 64:(tt + 1) * 64]])
            if l1_stop == "xv" and kind == "cross" and hh == 0:
                ph_dump(kb, B, VH, cast=True)
                ph_dump(kb, B, KHt, cast=True, col0=2048)
                ph_dump(kb, B, QH, cast=True, col0=4096)
                return True
            for qi in range(NT1):
                nkt = qi + 1 if kind == "self" else min(qi + 2, NT1)
                kx = nkt * 128
                for k5 in range((kx + 511) // 512):
                    w = min(512, kx - k5 * 512)
                    kb.op("tensor", lambda te, qi=qi, k5=k5, w=w: te.matmul(
                        B.psS[:, k5 * 512:k5 * 512 + w], QH[:, qi * 128:(qi + 1) * 128],
                        KHt[:, k5 * 512:k5 * 512 + w], start=True, stop=True),
                        reads=[QH[:, qi * 128:(qi + 1) * 128], KHt[:, k5 * 512:k5 * 512 + w]],
                        writes=[B.psS[:, k5 * 512:k5 * 512 + w]])
                mslice = B.msk[:, 0:128] if kind == "self" else B.msk[:, 128:256]
                kb.op("vector", lambda v, qi=qi, mslice=mslice: v.tensor_tensor(
                    B.psS[:, qi * 128:(qi + 1) * 128], B.psS[:, qi * 128:(qi + 1) * 128],
                    mslice, OP.add),
                    reads=[B.psS[:, qi * 128:(qi + 1) * 128], B.msk[:, :]],
                    writes=[B.psS[:, qi * 128:(qi + 1) * 128]])
                if kind == "cross" and nkt == qi + 2:
                    kb.op("vector", lambda v, qi=qi: v.tensor_tensor(
                        B.psS[:, (qi + 1) * 128:(qi + 2) * 128],
                        B.psS[:, (qi + 1) * 128:(qi + 2) * 128], B.msk[:, 256:384], OP.add),
                        reads=[B.psS[:, (qi + 1) * 128:(qi + 2) * 128], B.msk[:, :]],
                        writes=[B.psS[:, (qi + 1) * 128:(qi + 2) * 128]])
                nm = B.NM[:, 0:1]
                kb.op("vector", lambda v, kx=kx: v.reduce_max(nm, B.psS[:, 0:kx], axis=AX),
                      reads=[B.psS[:, 0:2048]], writes=[nm])
                kb.op("vector", lambda v: v.tensor_scalar(nm, nm, -0.125, None, OP.mult),
                      reads=[nm], writes=[nm])
                kb.op("scalar", lambda sc, kx=kx: sc.activation(
                    PQ[:, 0:kx], B.psS[:, 0:kx], AF.Exp, bias=nm, scale=0.125),
                    reads=[B.psS[:, 0:2048], nm], writes=[PQ])
                for kt in range(nkt):
                    kb.op("tensor", lambda te, kt=kt: te.transpose(
                        B.psB.bitcast(BF16)[:, 0:128], PQ[:, kt * 128:(kt + 1) * 128],
                        B.idnb[:, :]),
                        reads=[PQ, B.idnb[:, :]], writes=[B.psB[:, 0:64]])
                    kb.op("vector", lambda v, kt=kt: v.tensor_copy(
                        PTb[:, kt * 128:(kt + 1) * 128], B.psB.bitcast(BF16)[:, 0:128]),
                        reads=[B.psB[:, 0:64]], writes=[PTb[:, kt * 128:(kt + 1) * 128]])
                for kt in range(nkt):
                    eb = B.onesc[:, 0:1] if kind == "self" else B.SSCb[:, kt:kt + 1]
                    kb.op("tensor", lambda te, kt=kt, eb=eb, nkt=nkt: te.matmul(
                        B.psA[0:1, 0:128], eb, PTb[:, kt * 128:(kt + 1) * 128],
                        start=(kt == 0), stop=(kt == nkt - 1), skip_group_check=True),
                        reads=[eb, B.SSCb[:, :], PTb[:, kt * 128:(kt + 1) * 128]],
                        writes=[B.psA[:, 0:128]])
                for kt in range(nkt):
                    kb.op("tensor", lambda te, kt=kt, nkt=nkt: te.matmul(
                        B.psA[0:64, 128:256], VH[:, kt * 64:(kt + 1) * 64],
                        PTb[:, kt * 128:(kt + 1) * 128], start=(kt == 0), stop=(kt == nkt - 1),
                        skip_group_check=True),
                        reads=[VH[:, kt * 64:(kt + 1) * 64], PTb[:, kt * 128:(kt + 1) * 128]],
                        writes=[B.psA[:, 128:256]])
                kb.op("vector", lambda v: v.reciprocal(B.ZR[0:1, 0:128], B.psA[0:1, 0:128]),
                      reads=[B.psA[:, 0:128]], writes=[B.ZR[0:1, 0:128]])
                kb.op("tensor", lambda te: te.matmul(
                    B.psB[0:64, 0:128], B.ones1f[0:1, 0:64], B.ZR[0:1, 0:128],
                    start=True, stop=True),
                    reads=[B.ones1f[:, :], B.ZR[0:1, 0:128]], writes=[B.psB[:, 0:128]])
                rz = B.SC[:, 0:128]
                kb.op("scalar", lambda sc: sc.activation(rz[0:64, :], B.psB[0:64, 0:128], AF.Copy),
                      reads=[B.psB[:, 0:128]], writes=[rz])
                kb.op("vector", lambda v, qi=qi: v.tensor_tensor(
                    OUTH[:, qi * 128:(qi + 1) * 128], B.psA[0:64, 128:256], rz[0:64, :],
                    OP.mult),
                    reads=[B.psA[:, 128:256], rz], writes=[OUTH[:, qi * 128:(qi + 1) * 128]])
            wo = B.WT[0:64, 1536 + (hh % 2) * 1024:1536 + (hh % 2) * 1024 + 1024]
            kb.op("gpsimd", lambda g, hh=hh, pw=pw, wo=wo: g.dma_start(
                out=wo, in_=pw[hh * 64:(hh + 1) * 64, :]),
                reads=[], writes=[wo], dma=True)
            for oc in range(NC):
                for jt in range(NT5):
                    kb.op("tensor", lambda te, oc=oc, jt=jt, wo=wo: te.matmul(
                        B.psA[:, :], wo[:, oc * 128:oc * 128 + 128],
                        OUTH[:, jt * 512:(jt + 1) * 512], start=True, stop=True),
                        reads=[wo, OUTH[:, jt * 512:(jt + 1) * 512]], writes=[B.psA[:, 0:512]])
                    if hh == HEADS - 1:
                        kb.op("vector", lambda v, oc=oc, jt=jt: v.scalar_tensor_tensor(
                            fm(H, oc, jt), B.psA[:, :], B.BIAS[:, bslot + oc:bslot + oc + 1],
                            fm(H, oc, jt), OP.add, OP.add),
                            reads=[B.psA[:, 0:512], B.BIAS[:, 0:120], fm(H, oc, jt)],
                            writes=[fm(H, oc, jt)])
                    else:
                        kb.op("vector", lambda v, oc=oc, jt=jt: v.tensor_tensor(
                            fm(H, oc, jt), B.psA[:, :], fm(H, oc, jt), OP.add),
                            reads=[B.psA[:, 0:512], fm(H, oc, jt)], writes=[fm(H, oc, jt)])

    def mlp_sublayer():
        ph_layernorm(kb, B, f"{t}mlpln", src=lambda c, j: fm(H, c, j),
                     dst_bf=lambda c, j: fm(B.TA, c, j),
                     g_ap=B.LNP[:, 4 * NC:5 * NC], b_ap=B.LNP[:, 5 * NC:6 * NC])
        kb.settag(t + "mlp")
        Wfc = B.WT[:, 0:8192]
        Wpj = B.WT[:, 8192:16384]
        for sl in range(4):
            kb.op("gpsimd", lambda g, sl=sl: g.dma_start(
                out=Wfc.rearrange("p (c m) -> p c m", c=NC),
                in_=B.fcw[:, sl * 1024:(sl + 1) * 1024].rearrange("(c p) m -> p c m", p=128)),
                reads=[], writes=[Wfc], dma=True)
            kb.op("gpsimd", lambda g, sl=sl: g.dma_start(
                out=Wpj.rearrange("p (c m) -> p c m", c=NC),
                in_=B.mprojw[sl * 1024:(sl + 1) * 1024, :].rearrange("(c p) m -> p c m", p=128)),
                reads=[], writes=[Wpj], dma=True)
            for jt in range(NT5):
                for hc in range(NC):
                    for kc in range(NC):
                        kb.op("tensor", lambda te, kc=kc, hc=hc, jt=jt: te.matmul(
                            B.psA[:, :], Wfc[:, kc * 1024 + hc * 128:kc * 1024 + hc * 128 + 128],
                            fm(B.TA, kc, jt), start=(kc == 0), stop=(kc == NC - 1)),
                            reads=[Wfc, fm(B.TA, kc, jt)], writes=[B.psA[:, 0:512]])
                    hb = B.SC[:, 0:512]
                    kb.op("scalar", lambda sc, sl=sl, hc=hc: sc.activation(
                        hb, B.psA[:, :], AF.Identity, bias=B.BIAS[:, 72 + sl * 8 + hc:72 + sl * 8 + hc + 1]),
                        reads=[B.psA[:, 0:512], B.BIAS[:, 0:120]], writes=[hb])
                    sq = B.SC[:, 512:1024]
                    kb.op("scalar", lambda sc: sc.activation(sq, hb, AF.Square),
                          reads=[hb], writes=[sq])
                    w1 = B.SR[:, 0:512]
                    kb.op("vector", lambda v: v.tensor_scalar(w1, sq, 0.044715, 1.0, OP.mult, OP.add),
                          reads=[sq], writes=[w1])
                    u = B.SR[:, 512:1024]
                    kb.op("vector", lambda v: v.tensor_mul(u, hb, w1), reads=[hb, w1], writes=[u])
                    kb.op("scalar", lambda sc: sc.activation(u, u, AF.Sigmoid, scale=1.5957691216057308),
                          reads=[u], writes=[u])
                    gg = B.TC[:, hc * 512:(hc + 1) * 512]
                    kb.op("vector", lambda v, gg=gg: v.tensor_mul(gg, hb, u),
                          reads=[hb, u], writes=[gg])
                for oc in range(NC):
                    for hc in range(NC):
                        gg = B.TC[:, hc * 512:(hc + 1) * 512]
                        kb.op("tensor", lambda te, hc=hc, oc=oc, gg=gg: te.matmul(
                            B.psB[:, :], Wpj[:, hc * 1024 + oc * 128:hc * 1024 + oc * 128 + 128],
                            gg, start=(hc == 0), stop=(hc == NC - 1)),
                            reads=[Wpj, gg], writes=[B.psB[:, 0:512]])
                    if sl == 3:
                        kb.op("vector", lambda v, oc=oc, jt=jt: v.scalar_tensor_tensor(
                            fm(H, oc, jt), B.psB[:, :], B.BIAS[:, 104 + oc:104 + oc + 1],
                            fm(H, oc, jt), OP.add, OP.add),
                            reads=[B.psB[:, 0:512], B.BIAS[:, 0:120], fm(H, oc, jt)],
                            writes=[fm(H, oc, jt)])
                    else:
                        kb.op("vector", lambda v, oc=oc, jt=jt: v.tensor_tensor(
                            fm(H, oc, jt), B.psB[:, :], fm(H, oc, jt), OP.add),
                            reads=[B.psB[:, 0:512], fm(H, oc, jt)], writes=[fm(H, oc, jt)])

    attn_sublayer("self")
    if l1_stop == "self":
        return True
    r = attn_sublayer("cross")
    if r or l1_stop == "cross":
        return True
    mlp_sublayer()
    return False


def ph_spill_x(kb, B, s):
    kb.settag(f"sp{s}")
    kb.op("sync", lambda sy: sy.dma_start(
        out=B.spill[:, :].rearrange("(c p) n -> p c n", p=128),
        in_=B.X[:, :].rearrange("p (c n) -> p c n", c=NC)),
        reads=[B.X[:, :]], writes=[B.spill], dma=True)


def ph_unspill_add(kb, B, s):
    kb.settag(f"us{s}")
    for c in range(NC):
        for j in range(NT5):
            stg = B.SC[:, 0:512]
            kb.op("sync", lambda sy, c=c, j=j, stg=stg: sy.dma_start(
                out=stg, in_=B.spill[c * 128:(c + 1) * 128, j * 512:(j + 1) * 512]),
                reads=[B.spill], writes=[stg], dma=True)
            kb.op("vector", lambda v, c=c, j=j, stg=stg: v.tensor_tensor(
                fm(B.X, c, j), fm(B.X, c, j), stg, OP.add),
                reads=[fm(B.X, c, j), stg], writes=[fm(B.X, c, j)])



from concourse.bass_utils import run_bass_kernel_spmd


def kernel(**inputs):
    st, xT = stage_host(inputs)
    nc, B, _ = build(upto="all", dump=None, nseq=2)
    in_maps = []
    for c in range(8):
        m = dict(st)
        m["xT"] = np.ascontiguousarray(xT[2 * c:2 * c + 2])
        in_maps.append(m)
    res = run_bass_kernel_spmd(nc, in_maps, core_ids=list(range(8)))
    out = np.empty((16, 2048, 1024), np.float32)
    for c in range(8):
        o = np.asarray(res.results[c]["outT"])
        out[2 * c:2 * c + 2] = o.transpose(0, 2, 1)
    return out
